# version 05 (final)
# baseline (speedup 1.0000x reference)
import math
import numpy as np
import ml_dtypes
import concourse.bass as bass
import concourse.mybir as mybir
from concourse.bass_utils import run_bass_kernel_spmd

F32 = mybir.dt.float32
BF16 = mybir.dt.bfloat16
I8 = mybir.dt.int8
AF = mybir.ActivationFunctionType
ALU = mybir.AluOpType
AXX = mybir.AxisListType.X

B, N, D, H, HD = 2, 8192, 2048, 16, 128
HF, RK = 8192, 16
NC = 8
NP = 4
HPC = 4
CW = HPC * HD
GT = 1024
NG = N // GT
CH = 128
CPG = GT // CH
TB = 1024
NTB = 2048 // TB
KT = D // 128
HG = HF // 128
DG = D // 128
WCATC = 3 * CW + HPC + 128
EPS_LN = 1e-5
EPS_K = 1e-6


def _lowbyte(x_i32):
    return np.ascontiguousarray(x_i32.view(np.int8)[:, 0::4])


def host_prep(inp):
    asc = np.ascontiguousarray
    x = np.asarray(inp["L0_attn_x"], np.float32)
    xf = x.reshape(B * N, D)
    WqT = asc(inp["L0_attn_Wq"].T)
    WkT = asc(inp["L0_attn_Wk"].T)
    WvT = asc(inp["L0_attn_Wv"].T)
    WbT = asc(inp["L0_attn_Wbeta"].T)
    Wg1T = asc(inp["L0_attn_Wg1"].T)
    Wg2T = asc(np.asarray(inp["L0_attn_Wg2"], np.float32).T)
    gn = np.asarray(inp["L0_attn_g_norm"], np.float32)
    WoT = asc(np.asarray(inp["L0_attn_Wo"], np.float32).T)
    iup = asc(_lowbyte(np.asarray(inp["L0_ffn_w_up_q"], np.int32)).T)
    sup_exp = asc(np.repeat(
        asc(np.asarray(inp["L0_ffn_w_up_scale"], np.float32).T), 64, axis=0))
    bup = np.asarray(inp["L0_ffn_b_up"], np.float32)
    AupT = asc(np.asarray(inp["L0_ffn_w_up_lora_a"], np.float32).T)
    Bup = np.asarray(inp["L0_ffn_w_up_lora_b"], np.float32)
    idn = asc(_lowbyte(np.asarray(inp["L0_ffn_w_down_q"], np.int32)).T)
    sdn_exp = asc(np.repeat(
        asc(np.asarray(inp["L0_ffn_w_down_scale"], np.float32).T), 64, axis=0))
    bdn = np.asarray(inp["L0_ffn_b_down"], np.float32)
    AdnT = asc(np.asarray(inp["L0_ffn_w_down_lora_a"], np.float32).T)
    Bdn = np.asarray(inp["L0_ffn_w_down_lora_b"], np.float32)

    wcat = np.concatenate([np.concatenate([
        WqT[:, p * CW:(p + 1) * CW], WkT[:, p * CW:(p + 1) * CW],
        WvT[:, p * CW:(p + 1) * CW], WbT[:, p * HPC:(p + 1) * HPC], Wg1T],
        axis=1) for p in range(NP)], axis=1)
    lcat = asc(np.concatenate([AupT, Bup, AdnT, Bdn], axis=1))
    bcat = asc(np.concatenate([
        np.broadcast_to(gn, (128, D)),
        bup.reshape(HG, 128).T, bdn.reshape(DG, 128).T], axis=1))

    xAb = [asc(x[b]) for b in range(B)]
    in_maps = []
    for c in range(NC):
        m = {
            "xA": xAb[c // 4],
            "xB": asc(xf[c * 2048:(c + 1) * 2048]),
            "wcat": wcat,
            "wg2T": Wg2T,
            "woT": WoT,
            "iup": iup,
            "idn": idn,
            "sup": sup_exp,
            "sdn": sdn_exp,
            "lcat": lcat,
            "bcat": bcat,
        }
        in_maps.append(m)
    return in_maps


def _consts():
    i = np.arange(128)
    bd = (i[:, None] // 32) == (i[None, :] // 32)
    stril = i[:, None] > i[None, :]
    striu = i[:, None] < i[None, :]
    m = [np.eye(128),
         -(bd & stril).astype(np.float64),
         -(bd & striu).astype(np.float64),
         -(striu & ~bd).astype(np.float64),
         (i[:, None] <= i[None, :]).astype(np.float64)]
    return np.concatenate(m, axis=0).astype(ml_dtypes.bfloat16)


class Sched:

    ENGS = ("t", "v", "s", "q", "g")
    INC = {"t": 1, "v": 1, "s": 1, "q": 16, "g": 16}
    NSEM = 8

    def __init__(self):
        self.ops = []
        self.cnt = {e: 0 for e in self.ENGS}
        self.dcnt = {"q": 0, "g": 0}
        self.lastw = {}

    def op(self, eng, fn, deps=(), inc=None):
        if inc is None:
            inc = self.INC[eng]
        extra = []
        if eng in ("q", "g") and inc:
            i = self.dcnt[eng]
            self.dcnt[eng] += 1
            sem_key = f"{eng}{i % self.NSEM}"
            val = 16 * (i // self.NSEM + 1)
            if val > 16:
                extra.append((sem_key, val - 16))
            tok = (sem_key, val)
        else:
            sem_key = eng
            self.cnt[eng] += inc
            tok = (eng, self.cnt[eng])
        waits = []
        need = {}
        for d in list(deps) + extra:
            if d is None:
                continue
            key, val = d
            if key == eng == "t":
                continue
            need[key] = max(need.get(key, 0), val)
        for key, val in sorted(need.items()):
            if self.lastw.get((eng, key), 0) < val:
                waits.append((key, val))
                self.lastw[(eng, key)] = val
        self.ops.append((eng, fn, inc, waits, sem_key))
        return tok

    def qbar(self, eng):
        toks = []
        n = self.dcnt[eng]
        for j in range(self.NSEM):
            k = n - 1 - j
            if k < 0:
                break
            toks.append((f"{eng}{k % self.NSEM}", 16 * (k // self.NSEM + 1)))
        return toks

    def emit(self, nc, blk, sems):
        def runner(ename):
            def run(eng):
                for (oe, fn, inc, waits, sem_key) in self.ops:
                    if oe != ename:
                        continue
                    for (key, val) in waits:
                        eng.wait_ge(sems[key], val)
                    inst = fn(eng)
                    if inc:
                        inst.then_inc(sems[sem_key], inc)
            return run
        blk.sync(runner("q"))
        blk.tensor(runner("t"))
        blk.vector(runner("v"))
        blk.scalar(runner("s"))
        blk.gpsimd(runner("g"))


def _sbuf_map(nc):
    T = {}
    pos = [16640]

    def at(name, shape, dtype, off=None):
        if off is None:
            off = pos[0]
            sz = shape[1] * {F32: 4, BF16: 2, I8: 1}[dtype]
            pos[0] = (off + sz + 31) & ~31
        T[name] = nc.alloc_sbuf_tensor_at(name, shape, dtype, offset=off)
        return off

    for nm in ("ident", "mRd", "mRTd", "mRTf", "mLT"):
        at(nm, [128, 128], BF16)
    at("gnb", [128, CW], F32)
    at("bupc", [128, HG], F32)
    at("bdnc", [128, DG], F32)
    at("epsln", [128, 1], F32)
    at("epsk", [128, 1], F32)
    A0 = (pos[0] + 63) & ~63

    pos[0] = A0
    offs = {}
    at("lnT", [128, KT * GT], BF16)
    at("wq", [128, KT * CW], BF16)
    at("wk", [128, KT * CW], BF16)
    at("wv", [128, KT * CW], BF16)
    at("wg1", [128, KT * 128], BF16)
    at("wg2", [128, CW], BF16)
    at("wbeta", [128, KT * HPC], BF16)
    offs["xstage"] = at("xstage", [128, 2 * D], F32)
    at("lnstage", [128, 2 * D], BF16)
    offs["kv"] = at("kv", [128, CPG * HPC * 256], BF16)
    offs["Qt"] = at("Qt", [128, HPC * GT], BF16)
    at("G1", [128, GT], BF16)
    at("gate", [128, CPG * CW], BF16)
    at("beta", [128, CPG * HPC], F32)
    offs["ogT"] = at("ogT", [128, HPC * GT], BF16)
    at("ogs1", [128, CW], F32)
    at("ogs2", [128, CW], F32)
    at("rms", [128, 16], F32)
    at("lnst", [128, 2 * 64], F32)
    at("lncol", [128, 8], F32)
    for h in range(HPC):
        at(f"KT2_{h}", [128, 256], BF16)
        at(f"Zb_{h}", [128, 256], BF16)
        at(f"Rd_{h}", [128, 128], BF16)
        at(f"RTd_{h}", [128, 128], BF16)
        at(f"RTf_{h}", [128, 128], BF16)
        at(f"Xp0_{h}", [128, 256], BF16)
        at(f"Xp1_{h}", [128, 256], BF16)
        at(f"PT0_{h}", [128, 128], BF16)
        at(f"PT1_{h}", [128, 128], BF16)
        at(f"WU_{h}", [128, 256], BF16)
        at(f"tmp_{h}", [128, 256], BF16)
        at(f"WTn_{h}", [128, 128], BF16)
        at(f"Dd_{h}", [128, 128], BF16)
        at(f"LT_{h}", [128, 128], BF16)
        at(f"Ssb_{h}", [128, 128], BF16)
        at(f"S32_{h}", [128, 128], F32)
    at("ksq", [128, CW], F32)
    at("ogb", [128, CW], BF16)
    a_end = pos[0]

    pos[0] = A0
    at("ogTs", [128, KT * TB], BF16)
    at("attnT", [128, KT * TB], BF16)
    at("ln2t", [128, KT * TB], BF16)
    at("woTs", [128, 2 * KT * 128], BF16)
    at("xBs", [128, 2 * D], F32)
    at("x1t", [128, 2 * D], F32)
    at("wups", [128, 2 * KT * 128], BF16)
    at("x2o", [128, 2 * TB], BF16)
    at("x2i", [128, 2 * TB], BF16)
    at("wdns", [128, 2 * 4 * 128], BF16)
    at("outt", [128, D], F32)
    at("x1r", [128, 2 * D], F32)
    at("ln2st", [128, 64], F32)
    at("ln2col", [128, 8], F32)
    at("ln2s", [128, D], BF16)
    b_end = pos[0]

    at("prep_w", [128, HF], BF16, offs["Qt"])
    at("prep_i8", [128, HF], I8, offs["kv"])
    at("prep_sc", [128, 4096], F32, offs["xstage"])
    at("prep_wo", [128, D], F32, offs["ogT"])
    pos[0] = max(a_end, b_end)
    at("prep_la", [16, 128], BF16)
    at("prep_lb0", [16, 512], BF16)
    at("prep_lb1", [16, 512], BF16)

    assert pos[0] <= 212576, (a_end, b_end, pos[0])
    return T


def build_program():
    nc = bass.Bass("TRN2", target_bir_lowering=False)

    din = {}

    def dn(name, shape, dtype):
        din[name] = nc.dram_tensor(name, shape, dtype, kind="ExternalInput")
        return din[name]
    xA = dn("xA", [N, D], F32)
    xB = dn("xB", [2048, D], F32)
    wcat_d = dn("wcat", [D, NP * WCATC], F32)
    wg2T_d = dn("wg2T", [128, D], F32)
    woT_d = dn("woT", [D, D], F32)
    iup_d = dn("iup", [D, HF], I8)
    idn_d = dn("idn", [HF, D], I8)
    sup_d = dn("sup", [D, HF], F32)
    sdn_d = dn("sdn", [HF, D], F32)
    lcat_d = dn("lcat", [RK, 2 * D + 2 * HF], F32)
    bcat_d = dn("bcat", [128, D + HG + DG], F32)

    out_d = nc.dram_tensor("out", [2048, D], F32, kind="ExternalOutput")

    og_sp = nc.dram_tensor("og_sp", [D, N], BF16, kind="Internal")
    lnsp = nc.dram_tensor("lnsp", [D, N], BF16, kind="Internal")
    wag_d = nc.dram_tensor("wag", [2304, 128 * 128], BF16, kind="Internal")
    wup_ag = wag_d[0:1024, :]
    wdn_ag = wag_d[1024:2048, :]
    wo_ag = wag_d[2048:2304, :]
    x1sp = nc.dram_tensor("x1sp", [2048, D], F32, kind="Internal")
    x2sp = nc.dram_tensor("x2sp", [HF, TB], BF16, kind="Internal")

    cst_d = nc.inline_tensor(_consts(), "cst")

    T = _sbuf_map(nc)
    S = Sched()

    pb = [nc.alloc_psum_tensor(f"pb{i}", [128, 512], F32) for i in range(8)]

    t_setup = []
    for i_, nm in enumerate(("ident", "mRd", "mRTd", "mRTf", "mLT")):
        t_setup.append(S.op("g", lambda e, i_=i_, nm=nm: e.dma_start(
            T[nm][:], cst_d[i_ * 128:(i_ + 1) * 128, :])))
    S.op("g", lambda e: e.dma_start(T["bupc"][:], bcat_d[:, D:D + HG]))
    S.op("g", lambda e: e.dma_start(T["bdnc"][:], bcat_d[:, D + HG:D + HG + DG]))
    S.op("v", lambda e: e.memset(T["epsln"][:], EPS_LN))
    S.op("v", lambda e: e.memset(T["epsk"][:], EPS_K))

    symoff = {}

    def _q_init(e):
        pid = e.partition_id()
        r0 = e.alloc_register("ogoff0")
        r1 = e.alloc_register("ogoff1")
        e.reg_alu(r0, pid, 4, ALU.mod)
        e.reg_alu(r0, r0, 2048, ALU.mult)
        e.reg_mov(r1, r0)
        e.reg_alu(r1, r1, TB, ALU.add)
        RV = type(pid)
        symoff[0] = RV(val=r0, min_val=0, max_val=3 * 2048)
        symoff[1] = RV(val=r1, min_val=TB, max_val=3 * 2048 + TB)
        return e.nop()

    S.op("q", _q_init, inc=0)

    last = {}
    cons = {}

    def dep(name):
        return last.get(name)

    def use(name, tok):
        if tok is None:
            return
        d = cons.setdefault(name, {})
        e, v = tok
        d[e] = max(d.get(e, 0), v)

    def war(name):
        d = cons.pop(name, None)
        if not d:
            return []
        return [(e, v) for e, v in d.items()]

    def prod(name, eng, fn, deps=()):
        t = S.op(eng, fn, deps=list(deps) + [dep(name)] + war(name))
        last[name] = t
        return t

    def bk_use(i, tok):
        use(f"bk{i}", tok)

    def bk_war(i):
        return war(f"bk{i}")

    ALIASES = {
        "prep_w0": ["QtA", "G1A"] + [f"gate{t}" for t in range(8)],
        "prep_w1": ["QtA", "G1A"] + [f"gate{t}" for t in range(8)],
        "prep_i8": ["kv0", "kv1", "kv2", "kv3"],
        "prep_sc": ["xs0", "xs1"],
        "prep_wo": ["ogTA"],
    }

    def aprod(name, eng, fn, deps=()):
        al = ALIASES.get(name, [])
        dl = list(deps) + [dep(name)] + war(name)
        for a in al:
            dl += [dep(a)] + war(a)
        t = S.op(eng, fn, deps=dl)
        last[name] = t
        for a in al:
            use(a, t)
        return t

    def ause(name, tok):
        use(name, tok)
        for a in ALIASES.get(name, []):
            use(a, tok)

    up_toks, dn_toks, wo_toks = [], [], []

    def prep_up_slice(s):
        t_i8 = aprod("prep_i8", "g", lambda e, s=s: e.dma_start(
            T["prep_i8"][:], iup_d[s * 128:(s + 1) * 128, :]))
        for hh in range(2):
            t_sc = aprod("prep_sc", "g", lambda e, s=s, hh=hh: e.dma_start(
                T["prep_sc"][:], sup_d[s * 128:(s + 1) * 128,
                                       hh * 4096:(hh + 1) * 4096]))
            t_stt = aprod(f"prep_w{hh}", "v", lambda e, hh=hh:
                          e.scalar_tensor_tensor(
                              T["prep_w"][:, hh * 4096:(hh + 1) * 4096],
                              T["prep_i8"][:, hh * 4096:(hh + 1) * 4096], -7.5,
                              T["prep_sc"][:], ALU.add, ALU.mult),
                          deps=[t_i8, t_sc])
            ause("prep_i8", t_stt); ause("prep_sc", t_stt)
        t_la = prod("prep_la", "g", lambda e, s=s: e.dma_start(
            T["prep_la"][:], lcat_d[:, s * 128:(s + 1) * 128]))
        for ch in range(16):
            bi = ch % 2
            bank = pb[bi]
            lb = f"prep_lb{bi}"
            t_lb = prod(lb, "g", lambda e, ch=ch, lb=lb: e.dma_start(
                T[lb][:], lcat_d[:, D + ch * 512:D + (ch + 1) * 512]))
            t_mm = S.op("t", lambda e, s=s, ch=ch, bank=bank, lb=lb: e.matmul(
                bank[:, 0:512], T["prep_la"][0:16, :], T[lb][0:16, :],
                start=True, stop=True),
                deps=[t_la, t_lb, dep(f"prep_w{ch // 8}"),
                      dep(f"pfold{bi}")] + bk_war(bi))
            use("prep_la", t_mm); use(lb, t_mm)
            t_add = S.op("v", lambda e, ch=ch, bank=bank: e.tensor_tensor(
                T["prep_w"][:, ch * 512:(ch + 1) * 512], bank[:, 0:512],
                T["prep_w"][:, ch * 512:(ch + 1) * 512], ALU.add),
                deps=[t_mm, dep(f"prep_w{ch // 8}")])
            last[f"pfold{bi}"] = t_add
            ause(f"prep_w{ch // 8}", t_add)
            bk_use(bi, t_add)
        t_wr = S.op("q", lambda e, s=s: e.dma_start(
            wup_ag[s * 64:(s + 1) * 64, :].rearrange("w (p f) -> p w f", p=128),
            T["prep_w"][:].rearrange("p (w f) -> p w f", f=128)),
            deps=[dep("pfold0"), dep("pfold1")])
        up_toks.append(t_wr)
        ause("prep_w0", t_wr); ause("prep_w1", t_wr)
        last["prep_w0"] = None
        last["prep_w1"] = None

    def prep_dn_slice(t):
        t_i8 = aprod("prep_i8", "g", lambda e, t=t: e.dma_start(
            T["prep_i8"][:, 0:2048], idn_d[t * 128:(t + 1) * 128, :]))
        t_sc = aprod("prep_sc", "g", lambda e, t=t: e.dma_start(
            T["prep_sc"][:, 0:2048], sdn_d[t * 128:(t + 1) * 128, :]))
        t_stt = aprod("prep_w0", "v", lambda e: e.scalar_tensor_tensor(
            T["prep_w"][:, 0:2048], T["prep_i8"][:, 0:2048], -7.5,
            T["prep_sc"][:, 0:2048], ALU.add, ALU.mult), deps=[t_i8, t_sc])
        ause("prep_i8", t_stt); ause("prep_sc", t_stt)
        t_la = prod("prep_la", "g", lambda e, t=t: e.dma_start(
            T["prep_la"][:], lcat_d[:, D + HF + t * 128:D + HF + (t + 1) * 128]))
        for ch in range(4):
            bi = ch % 2
            bank = pb[bi]
            lb = f"prep_lb{bi}"
            t_lb = prod(lb, "g", lambda e, ch=ch, lb=lb: e.dma_start(
                T[lb][:],
                lcat_d[:, D + 2 * HF + ch * 512:D + 2 * HF + (ch + 1) * 512]))
            t_mm = S.op("t", lambda e, t=t, ch=ch, bank=bank, lb=lb: e.matmul(
                bank[:, 0:512], T["prep_la"][0:16, :], T[lb][0:16, :],
                start=True, stop=True),
                deps=[t_la, t_lb, dep("prep_w0"),
                      dep(f"pfold{bi}")] + bk_war(bi))
            use("prep_la", t_mm); use(lb, t_mm)
            t_add = S.op("v", lambda e, ch=ch, bank=bank: e.tensor_tensor(
                T["prep_w"][:, ch * 512:(ch + 1) * 512], bank[:, 0:512],
                T["prep_w"][:, ch * 512:(ch + 1) * 512], ALU.add),
                deps=[t_mm, dep("prep_w0")])
            last[f"pfold{bi}"] = t_add
            ause("prep_w0", t_add)
            bk_use(bi, t_add)
        t_wr = S.op("q", lambda e, t=t: e.dma_start(
            wdn_ag[(t // 8) * 128 + (t % 8) * 16:
                   (t // 8) * 128 + (t % 8) * 16 + 16, :].rearrange(
                       "w (p f) -> p w f", p=128),
            T["prep_w"][:, 0:2048].rearrange("p (w f) -> p w f", f=128)),
            deps=[dep("pfold0"), dep("pfold1")])
        dn_toks.append(t_wr)
        ause("prep_w0", t_wr)
        last["prep_w0"] = None

    def prep_wo_slice(s):
        t_ld = aprod("prep_wo", "g", lambda e, s=s: e.dma_start(
            T["prep_wo"][:], woT_d[s * 128:(s + 1) * 128, :]))
        t_cp = aprod("prep_w0", "v", lambda e: e.tensor_copy(
            T["prep_w"][:, 0:2048], T["prep_wo"][:]), deps=[t_ld])
        ause("prep_wo", t_cp)
        t_wr = S.op("q", lambda e, s=s: e.dma_start(
            wo_ag[s * 16:(s + 1) * 16, :].rearrange("w (p f) -> p w f", p=128),
            T["prep_w"][:, 0:2048].rearrange("p (w f) -> p w f", f=128)),
            deps=[t_cp])
        wo_toks.append(t_wr)
        ause("prep_w0", t_wr)
        last["prep_w0"] = None

    prep_q = []
    for s_ in range(16):
        prep_q.append(lambda s_=s_: prep_up_slice(s_))
    for t_ in range(64):
        prep_q.append(lambda t_=t_: prep_dn_slice(t_))
    for s_ in range(16):
        prep_q.append(lambda s_=s_: prep_wo_slice(s_))
    PREP_PER_GROUP = (len(prep_q) + NP * NG - 1) // (NP * NG)

    def pop_prep():
        for _ in range(PREP_PER_GROUP):
            if prep_q:
                prep_q.pop(0)()

    def kv3(tg):
        return T["kv"][:, tg * 1024:(tg + 1) * 1024].rearrange(
            "p (h z) -> p h z", z=256)

    lnsp_tok = {}

    def ln_chain(g):
        for tg in range(8):
            sl = tg % 2
            xs = T["xstage"][:, sl * D:(sl + 1) * D]
            t_x = prod(f"xs{sl}", "q", lambda e, g=g, xs=xs, tg=tg: e.dma_start(
                xs, xA[g * GT + tg * 128: g * GT + (tg + 1) * 128, :]))
            st = T["lnst"][:, sl * 64: sl * 64 + 24]
            stm = st.rearrange("p (c k) -> p c k", k=3)[:, :, 1:2].squeeze(2)
            stv = st.rearrange("p (c k) -> p c k", k=3)[:, :, 2:3].squeeze(2)
            msq = T["lnst"][:, sl * 64 + 24: sl * 64 + 40]
            col = T["lncol"][:, sl * 4: sl * 4 + 4]
            t_bn = None
            for bi_ in range(4):
                t_bn = S.op("v", lambda e, xs=xs, st=st, bi_=bi_: e.bn_stats(
                    st[:, bi_ * 6:(bi_ + 1) * 6],
                    xs[:, bi_ * 512:(bi_ + 1) * 512]),
                    deps=[t_x, dep(f"lnstat{sl}"), dep(f"lnsA{sl}")])
            t1 = S.op("v", lambda e, stm=stm, col=col: e.tensor_reduce(
                col[:, 0:1], stm, AXX, ALU.add), deps=[t_bn])
            t2 = S.op("v", lambda e, stv=stv, col=col: e.tensor_reduce(
                col[:, 3:4], stv, AXX, ALU.add), deps=[t_bn])
            t3 = S.op("v", lambda e, stm=stm, msq=msq: e.tensor_tensor(
                msq[:, 0:8], stm, stm, ALU.mult), deps=[t_bn])
            t4 = S.op("v", lambda e, msq=msq: e.tensor_reduce(
                msq[:, 0:1], msq[:, 0:8], AXX, ALU.add), deps=[t3])
            t5 = S.op("v", lambda e, col=col: e.tensor_scalar(
                col[:, 0:1], col[:, 0:1], 1.0 / 8, None, ALU.mult), deps=[t1])
            t7 = S.op("v", lambda e, col=col: e.tensor_scalar(
                col[:, 3:4], col[:, 3:4], 1.0 / D, None, ALU.mult), deps=[t2])
            t8 = S.op("v", lambda e, msq=msq: e.tensor_scalar(
                msq[:, 0:1], msq[:, 0:1], 1.0 / 8, None, ALU.mult), deps=[t4])
            t9 = S.op("v", lambda e, col=col, msq=msq: e.tensor_tensor(
                col[:, 3:4], col[:, 3:4], msq[:, 0:1], ALU.add), deps=[t7, t8])
            t10 = S.op("v", lambda e, col=col: e.tensor_tensor(
                col[:, 1:2], col[:, 0:1], col[:, 0:1], ALU.mult), deps=[t5])
            t11 = S.op("v", lambda e, col=col: e.tensor_tensor(
                col[:, 3:4], col[:, 3:4], col[:, 1:2], ALU.subtract),
                deps=[t9, t10])
            t12 = S.op("s", lambda e, col=col: e.activation(
                col[:, 1:2], col[:, 3:4], AF.Sqrt, bias=T["epsln"][:, 0:1]),
                deps=[t11])
            t13 = S.op("v", lambda e, col=col: e.reciprocal(
                col[:, 1:2], col[:, 1:2]), deps=[t12])
            t14 = S.op("v", lambda e, col=col: e.tensor_tensor(
                col[:, 2:3], col[:, 0:1], col[:, 1:2], ALU.mult), deps=[t13])
            t15 = S.op("v", lambda e, col=col: e.tensor_scalar(
                col[:, 2:3], col[:, 2:3], -1.0, None, ALU.mult), deps=[t14])
            last[f"lnstat{sl}"] = t15
            lns = T["lnstage"][:, sl * D:(sl + 1) * D]
            t_ln = prod(f"lns{sl}", "s", lambda e, xs=xs, lns=lns, col=col:
                        e.activation(lns, xs, AF.Identity,
                                     bias=col[:, 2:3], scale=col[:, 1:2]),
                        deps=[t13, t15])
            last[f"lnsA{sl}"] = t_ln
            use(f"xs{sl}", t_ln)
            use(f"xs{sl}", t_bn)
            for kb in range(0, KT, 4):
                bank = pb[2] if ((tg * 4 + kb // 4) % 2 == 0) else pb[5]
                bi = 2 if bank is pb[2] else 5
                mm_last = None
                for j in range(4):
                    kt = kb + j
                    mm_last = S.op("t", lambda e, lns=lns, kt=kt, j=j, bank=bank:
                                   e.matmul(bank[:, j * 128:(j + 1) * 128],
                                            lns[:, kt * 128:(kt + 1) * 128],
                                            T["ident"][:], start=True, stop=True),
                                   deps=[t_ln, t_setup[0]] + bk_war(bi))
                use(f"lns{sl}", mm_last)
                dst = T["lnT"][:].rearrange("p (k z) -> p k z", z=GT)[
                    :, kb:kb + 4, tg * 128:(tg + 1) * 128]
                ev = prod("lnT", "v", lambda e, dst=dst, bank=bank: e.tensor_copy(
                    dst, bank[:, 0:512]), deps=[mm_last])
                bk_use(bi, ev)
        t_spill = S.op("q", lambda e, g=g: e.dma_start(
            lnsp[:, g * GT:(g + 1) * GT].rearrange("(k p) n -> p k n", p=128),
            T["lnT"][:].rearrange("p (k n) -> p k n", n=GT)),
            deps=[dep("lnT")])
        use("lnT", t_spill)
        lnsp_tok[g] = t_spill

    def ln_load(g):
        prod("lnT", "q", lambda e, g=g: e.dma_start(
            T["lnT"][:].rearrange("p (k n) -> p k n", n=GT),
            lnsp[:, g * GT:(g + 1) * GT].rearrange("(k p) n -> p k n", p=128)),
            deps=[lnsp_tok[g]])

    og_write_toks = []
    for p in range(NP):
        def wload(name, c0, k, p=p):
            return prod(name, "g", lambda e, c0=c0, k=k, p=p: e.dma_start(
                T[name][:].rearrange("p (t c) -> p t c", c=k),
                wcat_d[:, p * WCATC + c0: p * WCATC + c0 + k].rearrange(
                    "(t p) c -> p t c", p=128)))
        wload("wq", 0, CW)
        wload("wk", CW, CW)
        wload("wv", 2 * CW, CW)
        wload("wbeta", 3 * CW, HPC)
        wload("wg1", 3 * CW + HPC, 128)
        prod("wg2", "g", lambda e, p=p: e.dma_start(
            T["wg2"][:], wg2T_d[:, p * CW:(p + 1) * CW]))
        prod("gnb", "g", lambda e, p=p: e.dma_start(
            T["gnb"][:], bcat_d[:, p * CW:(p + 1) * CW]))
        for h in range(HPC):
            prod(f"Ssb_{h}", "v", lambda e, h=h: e.memset(T[f"Ssb_{h}"][:], 0.0))
            prod(f"S32_{h}", "v", lambda e, h=h: e.memset(T[f"S32_{h}"][:], 0.0))

        if p == 0:
            ln_chain(0)
        else:
            ln_load(0)

        for g in range(NG):
            pop_prep()
            for tg in range(8):
                ik = 0 if tg % 2 == 0 else 3
                iv = 1 if tg % 2 == 0 else 4
                kmm = vmm = bmm = None
                for kt in range(KT):
                    lt = T["lnT"][:, kt * GT + tg * 128: kt * GT + (tg + 1) * 128]
                    kmm = S.op("t", lambda e, lt=lt, ik=ik, kt=kt: e.matmul(
                        pb[ik][:, 0:512], lt, T["wk"][:, kt * CW:(kt + 1) * CW],
                        start=(kt == 0), stop=(kt == KT - 1)),
                        deps=[dep("lnT"), dep("wk")] + (bk_war(ik) if kt == 0 else []))
                    vmm = S.op("t", lambda e, lt=lt, iv=iv, kt=kt: e.matmul(
                        pb[iv][:, 0:512], lt, T["wv"][:, kt * CW:(kt + 1) * CW],
                        start=(kt == 0), stop=(kt == KT - 1)),
                        deps=[dep("wv")] + (bk_war(iv) if kt == 0 else []))
                    bmm = S.op("t", lambda e, lt=lt, tg=tg, kt=kt: e.matmul(
                        pb[6][:, tg * 4:tg * 4 + 4], lt,
                        T["wbeta"][:, kt * HPC:(kt + 1) * HPC],
                        start=(kt == 0), stop=(kt == KT - 1)),
                        deps=[dep("wbeta")] + (bk_war(6) if (tg == 0 and kt == 0) else []))
                use("lnT", kmm)
                use("wk", kmm); use("wv", vmm); use("wbeta", bmm)
                t_ke = prod(f"kv{tg}", "s", lambda e, tg=tg, ik=ik: e.activation(
                    kv3(tg)[:, :, 0:128],
                    pb[ik][:, 0:512].rearrange("p (h z) -> p h z", h=HPC),
                    AF.Silu), deps=[kmm])
                bk_use(ik, t_ke)
                t_ve = prod(f"kv{tg}", "v", lambda e, tg=tg, iv=iv: e.tensor_copy(
                    kv3(tg)[:, :, 128:256],
                    pb[iv][:, 0:512].rearrange("p (h z) -> p h z", h=HPC)),
                    deps=[vmm])
                bk_use(iv, t_ve)
                t_sq = S.op("v", lambda e, tg=tg: e.tensor_tensor(
                    T["ksq"][:, 0:CW].rearrange("p (h z) -> p h z", h=HPC),
                    kv3(tg)[:, :, 0:128], kv3(tg)[:, :, 0:128], ALU.mult),
                    deps=[t_ke, dep("ksqw")])
                last["ksqw"] = t_sq
                t_rd = S.op("v", lambda e: e.tensor_reduce(
                    T["rms"][:, 12:16],
                    T["ksq"][:, 0:CW].rearrange("p (h z) -> p h z", h=HPC),
                    AXX, ALU.add), deps=[t_sq, dep("rmsk")])
                t_sr = S.op("s", lambda e: e.activation(
                    T["rms"][:, 12:16], T["rms"][:, 12:16], AF.Sqrt,
                    bias=T["epsk"][:, 0:1]), deps=[t_rd])
                t_rc = S.op("v", lambda e: e.reciprocal(
                    T["rms"][:, 12:16], T["rms"][:, 12:16]), deps=[t_sr])
                last["rmsk"] = t_rc
                t_km = prod(f"kv{tg}", "v", lambda e, tg=tg: e.tensor_tensor(
                    kv3(tg)[:, :, 0:128], kv3(tg)[:, :, 0:128],
                    T["rms"][:, 12:16].unsqueeze(2).broadcast_to((128, HPC, 128)),
                    ALU.mult), deps=[t_rc])
                use("rmsk", t_km)
            t_be = S.op("s", lambda e: e.activation(
                T["beta"][:, 0:32], pb[6][:, 0:32], AF.Sigmoid),
                deps=[bmm] + war("betaA"))
            bk_use(6, t_be)
            prod("betaA", "v", lambda e: e.tensor_scalar(
                T["beta"][:, 0:32], T["beta"][:, 0:32], 2.0, None, ALU.mult),
                deps=[t_be])
            for h in range(HPC):
                iq = (0, 1) if h % 2 == 0 else (3, 4)
                qmm = None
                for kt in range(KT):
                    for half in range(2):
                        qmm = S.op("t", lambda e, kt=kt, h=h, half=half, iq=iq:
                                   e.matmul(pb[iq[half]][:, 0:512],
                                            T["wq"][:, kt * CW + h * 128: kt * CW + (h + 1) * 128],
                                            T["lnT"][:, kt * GT + half * 512: kt * GT + (half + 1) * 512],
                                            start=(kt == 0), stop=(kt == KT - 1)),
                                   deps=([dep("wq")] + (bk_war(iq[half]) if kt == 0 else [])))
                use("lnT", qmm)
                use("wq", qmm)
                for half in range(2):
                    t_qe = prod("QtA", "s", lambda e, h=h, half=half, iq=iq:
                                e.activation(
                                    T["Qt"][:, h * GT + half * 512: h * GT + (half + 1) * 512],
                                    pb[iq[half]][:, 0:512], AF.Silu), deps=[qmm])
                    bk_use(iq[half], t_qe)
            gmm = None
            for kt in range(KT):
                for half in range(2):
                    gmm = S.op("t", lambda e, kt=kt, half=half: e.matmul(
                        pb[half][:, 0:512], T["wg1"][:, kt * 128:(kt + 1) * 128],
                        T["lnT"][:, kt * GT + half * 512: kt * GT + (half + 1) * 512],
                        start=(kt == 0), stop=(kt == KT - 1)),
                        deps=([dep("wg1")] + (bk_war(half) if kt == 0 else [])))
            use("lnT", gmm)
            use("wg1", gmm)
            g1_toks = []
            for half in range(2):
                tt = prod("G1A", "v", lambda e, half=half: e.tensor_copy(
                    T["G1"][:, half * 512:(half + 1) * 512], pb[half][:, 0:512]),
                    deps=[gmm])
                bk_use(half, tt)
                g1_toks.append(tt)
            for tg in range(8):
                ig = 2 if tg % 2 == 0 else 5
                gt_mm = S.op("t", lambda e, tg=tg, ig=ig: e.matmul(
                    pb[ig][:, 0:512], T["G1"][:, tg * 128:(tg + 1) * 128],
                    T["wg2"][:], start=True, stop=True),
                    deps=g1_toks + [dep("wg2")] + bk_war(ig))
                use("G1A", gt_mm)
                use("wg2", gt_mm)
                t_ge = prod(f"gate{tg}", "s", lambda e, tg=tg, ig=ig: e.activation(
                    T["gate"][:, tg * CW:(tg + 1) * CW], pb[ig][:, 0:512],
                    AF.Sigmoid), deps=[gt_mm])
                bk_use(ig, t_ge)
            if g + 1 < NG:
                if p == 0:
                    ln_chain(g + 1)
                else:
                    ln_load(g + 1)
            for c in range(CPG):
                for h in range(HPC):
                    par = h % 2
                    b0, b1, b2 = (0, 1, 2) if par == 0 else (3, 4, 5)
                    kvh = T["kv"][:, c * 1024 + h * 256: c * 1024 + h * 256 + 256]
                    bcol = T["beta"][:, c * HPC + h: c * HPC + h + 1]
                    Zb = T[f"Zb_{h}"]
                    KT2 = T[f"KT2_{h}"]
                    qsl = T["Qt"][:, h * GT + c * 128: h * GT + (c + 1) * 128]
                    t_zb = prod(f"Zb_{h}", "v", lambda e, kvh=kvh, bcol=bcol, Zb=Zb:
                                e.tensor_scalar(Zb[:], kvh, bcol, None, ALU.mult),
                                deps=[dep(f"kv{c}"), dep("betaA")])
                    use(f"kv{c}", t_zb)
                    use("betaA", t_zb)
                    t_m1 = S.op("t", lambda e, kvh=kvh, b0=b0: e.matmul(
                        pb[b0][:, 0:128], kvh[:, 0:128], T["ident"][:],
                        start=True, stop=True),
                        deps=[dep(f"kv{c}"), t_setup[0]] + bk_war(b0))
                    use(f"kv{c}", t_m1)
                    t_m2 = S.op("t", lambda e, Zb=Zb, b0=b0: e.matmul(
                        pb[b0][:, 128:256], Zb[:, 0:128], T["ident"][:],
                        start=True, stop=True), deps=[t_zb])
                    use(f"Zb_{h}", t_m2)
                    t_kt2 = prod(f"KT2_{h}", "v", lambda e, KT2=KT2, b0=b0:
                                 e.tensor_copy(KT2[:], pb[b0][:, 0:256]),
                                 deps=[t_m2])
                    bk_use(b0, t_kt2)
                    t_a1 = S.op("t", lambda e, KT2=KT2, b0=b0: e.matmul(
                        pb[b0][:, 256:384], KT2[:, 128:256], KT2[:, 0:128],
                        start=True, stop=True), deps=[t_kt2])
                    t_a2 = S.op("t", lambda e, KT2=KT2, b0=b0: e.matmul(
                        pb[b0][:, 384:512], KT2[:, 0:128], KT2[:, 128:256],
                        start=True, stop=True), deps=[t_kt2])
                    use(f"KT2_{h}", t_a2)
                    t_rd_ = prod(f"Rd_{h}", "v", lambda e, h=h, b0=b0: e.tensor_tensor(
                        T[f"Rd_{h}"][:], pb[b0][:, 256:384], T["mRd"][:], ALU.mult),
                        deps=[t_a1, t_setup[1]])
                    t_rtd = prod(f"RTd_{h}", "v", lambda e, h=h, b0=b0: e.tensor_tensor(
                        T[f"RTd_{h}"][:], pb[b0][:, 384:512], T["mRTd"][:], ALU.mult),
                        deps=[t_a2, t_setup[2]])
                    t_rtf = prod(f"RTf_{h}", "v", lambda e, h=h, b0=b0: e.tensor_tensor(
                        T[f"RTf_{h}"][:], pb[b0][:, 384:512], T["mRTf"][:], ALU.mult),
                        deps=[t_a2, t_setup[3]])
                    bk_use(b0, t_rtd)
                    bk_use(b0, t_rtf)
                    t_pt = prod(f"PT0_{h}", "v", lambda e, h=h: e.tensor_tensor(
                        T[f"PT0_{h}"][:], T[f"RTd_{h}"][:], T["ident"][:], ALU.add),
                        deps=[t_rtd])
                    pt_cur = f"PT0_{h}"
                    xsrc = None
                    for j in range(1, 5):
                        if xsrc is None:
                            Xl, XTl = T[f"Rd_{h}"][:], T[f"RTd_{h}"][:]
                            dl = [dep(f"Rd_{h}"), dep(f"RTd_{h}")]
                        else:
                            Xl, XTl = T[xsrc][:, 0:128], T[xsrc][:, 128:256]
                            dl = [dep(xsrc)]
                        t_s1 = S.op("t", lambda e, Xl=Xl, XTl=XTl, b1=b1: e.matmul(
                            pb[b1][:, 0:128], XTl, Xl, start=True, stop=True),
                            deps=dl + bk_war(b1))
                        t_s2 = S.op("t", lambda e, Xl=Xl, XTl=XTl, b1=b1: e.matmul(
                            pb[b1][:, 128:256], Xl, XTl, start=True, stop=True),
                            deps=dl)
                        xdst = f"Xp{j % 2}_{h}"
                        t_se = prod(xdst, "v", lambda e, xdst=xdst, b1=b1:
                                    e.tensor_copy(T[xdst][:], pb[b1][:, 0:256]),
                                    deps=[t_s2])
                        bk_use(b1, t_se)
                        if xsrc is None:
                            use(f"Rd_{h}", t_s2); use(f"RTd_{h}", t_s2)
                        else:
                            use(xsrc, t_s2)
                        t_pm = S.op("t", lambda e, xdst=xdst, pt_cur=pt_cur, b1=b1:
                                    e.matmul(pb[b1][:, 256:384], T[xdst][:, 0:128],
                                             T[pt_cur][:], start=True, stop=True),
                                    deps=[t_se, dep(pt_cur)])
                        use(xdst, t_pm)
                        use(pt_cur, t_pm)
                        pt_new = f"PT{(j) % 2}_{h}"
                        t_pt = prod(pt_new, "v", lambda e, pt_new=pt_new,
                                    pt_cur=pt_cur, b1=b1: e.tensor_tensor(
                                        T[pt_new][:], pb[b1][:, 256:384],
                                        T[pt_cur][:], ALU.add),
                                    deps=[t_pm])
                        bk_use(b1, t_pt)
                        use(pt_cur, t_pt)
                        xsrc = xdst
                        pt_cur = pt_new
                    WU = T[f"WU_{h}"]
                    tmp = T[f"tmp_{h}"]
                    t_e = S.op("t", lambda e, Zb=Zb, pt_cur=pt_cur, b2=b2:
                               e.matmul(pb[b2][:, 0:256], T[pt_cur][:], Zb[:],
                                        start=True, stop=True),
                               deps=[t_pt, t_zb] + bk_war(b2))
                    use(pt_cur, t_e); use(f"Zb_{h}", t_e)
                    t_wu = prod(f"WU_{h}", "v", lambda e, WU=WU, b2=b2:
                                e.tensor_copy(WU[:], pb[b2][:, 0:256]),
                                deps=[t_e])
                    bk_use(b2, t_wu)
                    for it in range(3):
                        t_cw = S.op("t", lambda e, WU=WU, h=h, b2=b2: e.matmul(
                            pb[b2][:, 256:512], T[f"RTf_{h}"][:], WU[:],
                            start=True, stop=True),
                            deps=[dep(f"WU_{h}"), dep(f"RTf_{h}")] + bk_war(b2))
                        use(f"WU_{h}", t_cw); use(f"RTf_{h}", t_cw)
                        t_tmp = prod(f"tmp_{h}", "v", lambda e, tmp=tmp, Zb=Zb, b2=b2:
                                     e.tensor_tensor(tmp[:], pb[b2][:, 256:512],
                                                     Zb[:], ALU.add),
                                     deps=[t_cw, t_zb])
                        use(f"Zb_{h}", t_tmp)
                        bk_use(b2, t_tmp)
                        t_e2 = S.op("t", lambda e, tmp=tmp, pt_cur=pt_cur, b2=b2:
                                    e.matmul(pb[b2][:, 0:256], T[pt_cur][:], tmp[:],
                                             start=True, stop=True),
                                    deps=[t_tmp, t_pt] + bk_war(b2))
                        use(pt_cur, t_e2); use(f"tmp_{h}", t_e2)
                        t_wu = prod(f"WU_{h}", "v", lambda e, WU=WU, b2=b2:
                                    e.tensor_copy(WU[:], pb[b2][:, 0:256]),
                                    deps=[t_e2])
                        bk_use(b2, t_wu)
                    t_wt = S.op("t", lambda e, WU=WU, b0=b0: e.matmul(
                        pb[b0][:, 0:128], WU[:, 0:128], T["ident"][:],
                        start=True, stop=True),
                        deps=[dep(f"WU_{h}")] + bk_war(b0))
                    use(f"WU_{h}", t_wt)
                    t_wtn = prod(f"WTn_{h}", "v", lambda e, h=h, b0=b0: e.tensor_scalar(
                        T[f"WTn_{h}"][:], pb[b0][:, 0:128], -1.0, None, ALU.mult),
                        deps=[t_wt])
                    bk_use(b0, t_wtn)
                    t_dp = S.op("t", lambda e, h=h, b0=b0: e.matmul(
                        pb[b0][:, 128:256], T[f"WTn_{h}"][:], T[f"Ssb_{h}"][:],
                        start=True, stop=True),
                        deps=[t_wtn, dep(f"Ssb_{h}")])
                    use(f"WTn_{h}", t_dp); use(f"Ssb_{h}", t_dp)
                    t_d = prod(f"Dd_{h}", "v", lambda e, h=h, WU=WU, b0=b0:
                               e.tensor_tensor(T[f"Dd_{h}"][:], pb[b0][:, 128:256],
                                               WU[:, 128:256], ALU.add),
                               deps=[t_dp, dep(f"WU_{h}")])
                    use(f"WU_{h}", t_d)
                    bk_use(b0, t_d)
                    t_lp = S.op("t", lambda e, KT2=KT2, qsl=qsl, b0=b0: e.matmul(
                        pb[b0][:, 256:384], KT2[:, 0:128], qsl,
                        start=True, stop=True),
                        deps=[dep(f"KT2_{h}"), dep("QtA")] + bk_war(b0))
                    use(f"KT2_{h}", t_lp); use("QtA", t_lp)
                    t_lt = prod(f"LT_{h}", "v", lambda e, h=h, b0=b0: e.tensor_tensor(
                        T[f"LT_{h}"][:], pb[b0][:, 256:384], T["mLT"][:], ALU.mult),
                        deps=[t_lp, t_setup[4]])
                    bk_use(b0, t_lt)
                    t_o1 = S.op("t", lambda e, qsl=qsl, h=h: e.matmul(
                        pb[6][:, h * 128:(h + 1) * 128], qsl, T[f"Ssb_{h}"][:],
                        start=True, stop=False),
                        deps=[dep(f"Ssb_{h}"), dep("QtA")] + (bk_war(6) if h == 0 else []))
                    use(f"Ssb_{h}", t_o1); use("QtA", t_o1)
                    t_o2 = S.op("t", lambda e, h=h: e.matmul(
                        pb[6][:, h * 128:(h + 1) * 128], T[f"LT_{h}"][:], T[f"Dd_{h}"][:],
                        start=False, stop=True), deps=[t_lt, t_d])
                    use(f"LT_{h}", t_o2); use(f"Dd_{h}", t_o2)
                    last[f"omm_{h}"] = t_o2
                    t_su = S.op("t", lambda e, kvh=kvh, h=h:
                                e.matmul(pb[7][:, h * 128:(h + 1) * 128],
                                         kvh[:, 0:128], T[f"Dd_{h}"][:],
                                         start=True, stop=True),
                                deps=[t_d, dep(f"kv{c}")] + bk_war(7))
                    use(f"kv{c}", t_su); use(f"Dd_{h}", t_su)
                    t_s32 = prod(f"S32_{h}", "v", lambda e, h=h: e.tensor_tensor(
                        T[f"S32_{h}"][:], pb[7][:, h * 128:(h + 1) * 128],
                        T[f"S32_{h}"][:], ALU.add), deps=[t_su])
                    bk_use(7, t_s32)
                    t_ss = prod(f"Ssb_{h}", "v", lambda e, h=h: e.tensor_copy(
                        T[f"Ssb_{h}"][:], T[f"S32_{h}"][:]),
                        deps=[t_s32])
                    use(f"S32_{h}", t_ss)
                o_deps = [dep(f"omm_{h2}") for h2 in range(HPC)]
                t_og1 = S.op("v", lambda e, c=c: e.tensor_tensor(
                    T["ogs1"][:], pb[6][:, 0:512], T["gate"][:, c * CW:(c + 1) * CW],
                    ALU.mult), deps=o_deps + [dep(f"gate{c}"), dep("ogs1w")])
                use(f"gate{c}", t_og1)
                bk_use(6, t_og1)
                t_og2 = S.op("v", lambda e: e.tensor_tensor(
                    T["ogs2"][:], T["ogs1"][:], T["ogs1"][:], ALU.mult),
                    deps=[t_og1, dep("ogs2w")])
                t_og3 = S.op("v", lambda e: e.tensor_reduce(
                    T["rms"][:, 0:4], T["ogs2"][:].rearrange("p (h z) -> p h z", h=HPC),
                    AXX, ALU.add), deps=[t_og2, dep("rmsog")])
                t_og4 = S.op("v", lambda e: e.tensor_scalar(
                    T["rms"][:, 0:4], T["rms"][:, 0:4], 1.0 / HD, None, ALU.mult),
                    deps=[t_og3])
                t_og5 = S.op("s", lambda e: e.activation(
                    T["rms"][:, 4:8], T["rms"][:, 0:4], AF.Sqrt,
                    bias=T["epsk"][:, 0:1]), deps=[t_og4])
                t_og6 = S.op("v", lambda e: e.reciprocal(T["rms"][:, 4:8], T["rms"][:, 4:8]),
                             deps=[t_og5])
                last["rmsog"] = t_og6
                t_og7 = S.op("v", lambda e: e.tensor_tensor(
                    T["ogs2"][:].rearrange("p (h z) -> p h z", h=HPC),
                    T["ogs1"][:].rearrange("p (h z) -> p h z", h=HPC),
                    T["rms"][:, 4:8].unsqueeze(2).broadcast_to((128, HPC, 128)),
                    ALU.mult), deps=[t_og6])
                last["ogs2w"] = t_og7
                t_og8 = S.op("v", lambda e: e.tensor_tensor(
                    T["ogs1"][:], T["ogs2"][:], T["gnb"][:], ALU.mult),
                    deps=[t_og7, dep("gnb")])
                use("gnb", t_og8)
                last["ogs1w"] = t_og8
                t_og9 = S.op("v", lambda e: e.tensor_copy(
                    T["ogb"][:], T["ogs1"][:]), deps=[t_og8, dep("ogbw")])
                use("ogs1w", t_og9)
                ogb = T["ogb"][:]
                tr_toks = []
                for h in range(HPC):
                    tt = S.op("t", lambda e, h=h, ogb=ogb: e.matmul(
                        pb[6][:, h * 128:(h + 1) * 128], ogb[:, h * 128:(h + 1) * 128],
                        T["ident"][:], start=True, stop=True),
                        deps=[t_og9] + (bk_war(6) if h == 0 else []))
                    tr_toks.append(tt)
                last["ogbw"] = tr_toks[-1]
                dst = T["ogT"][:].rearrange("p (h z) -> p h z", z=GT)[
                    :, :, c * 128:(c + 1) * 128]
                t_oge = prod("ogTA", "v", lambda e, dst=dst: e.tensor_copy(
                    dst, pb[6][:, 0:512]), deps=[tr_toks[-1]])
                bk_use(6, t_oge)
            for h in range(HPC):
                tw = S.op("q", lambda e, g=g, h=h, p=p: e.dma_start(
                    og_sp[p * 512 + h * 128: p * 512 + (h + 1) * 128,
                          g * GT:(g + 1) * GT],
                    T["ogT"][:, h * GT:(h + 1) * GT]), deps=[dep("ogTA")])
                use("ogTA", tw)
                og_write_toks.append(tw)

    while prep_q:
        prep_q.pop(0)()

    barA = ([(e, S.cnt[e]) for e in ("t", "v", "s") if S.cnt[e]]
            + S.qbar("q") + S.qbar("g"))
    firstB = {"q": True, "t": True, "v": True, "s": True}

    def fb(eng):
        if firstB[eng]:
            firstB[eng] = False
            return list(barA)
        return []

    def sym_cols(e, t):
        return bass.DynSlice(symoff[t], TB)

    last_out = None
    x1sp_tok = {}
    x2sp_w_tok = {}
    x2sp_r_tok = {}
    for tb in range(NTB):
        ogl_last = prod("ogTs", "q", lambda e, tb=tb: e.dma_start(
            T["ogTs"][:].rearrange("p (k z) -> p k z", z=TB),
            og_sp[:].rearrange("(k p) n -> p k n", p=128)[:, :, sym_cols(e, tb)]),
            deps=og_write_toks + fb("q"))
        for dg in range(DG):
            sl = dg % 2
            t_wl = prod(f"woTs{sl}", "q", lambda e, dg=dg, sl=sl: e.dma_start(
                T["woTs"][:, sl * 2048:(sl + 1) * 2048].rearrange(
                    "p (r z f) -> p r z f", z=2, f=128),
                wo_ag[:].rearrange("(r z w) (p f) -> p r z w f", z=2, w=16, p=128)[
                    :, :, :, dg, :]),
                deps=wo_toks)
            mm_l = [None, None]
            for kt in range(KT):
                for half in range(2):
                    bi = 2 * sl + half
                    mm_l[half] = S.op("t", lambda e, kt=kt, half=half, sl=sl, bi=bi:
                                      e.matmul(
                                          pb[bi][:, 0:512],
                                          T["woTs"][:, sl * 2048 + kt * 128: sl * 2048 + (kt + 1) * 128],
                                          T["ogTs"][:, kt * TB + half * 512: kt * TB + (half + 1) * 512],
                                          start=(kt == 0), stop=(kt == KT - 1)),
                                      deps=([t_wl, ogl_last] + fb("t")
                                            + (bk_war(bi) if kt == 0 else [])))
            use(f"woTs{sl}", mm_l[1])
            use("ogTs", mm_l[1])
            for half in range(2):
                bi = 2 * sl + half
                ev = prod("attnT", "v", lambda e, dg=dg, half=half, bi=bi:
                          e.tensor_copy(
                              T["attnT"][:, dg * TB + half * 512: dg * TB + (half + 1) * 512],
                              pb[bi][:, 0:512]), deps=[mm_l[half]] + fb("v"))
                bk_use(bi, ev)
        for tg in range(8):
            sl = tg % 2
            t_xb = prod(f"xBs{sl}", "q", lambda e, tb=tb, tg=tg, sl=sl: e.dma_start(
                T["xBs"][:, sl * D:(sl + 1) * D],
                xB[tb * TB + tg * 128: tb * TB + (tg + 1) * 128, :]), deps=[])
            mm_t = {}
            for dg in range(DG):
                bi = 4 + dg // 4
                mm_t[bi] = S.op("t", lambda e, dg=dg, tg=tg, bi=bi: e.matmul(
                    pb[bi][:, (dg % 4) * 128:(dg % 4 + 1) * 128],
                    T["attnT"][:, dg * TB + tg * 128: dg * TB + (tg + 1) * 128],
                    T["ident"][:], start=True, stop=True),
                    deps=[dep("attnT")] + (bk_war(bi) if dg % 4 == 0 else []))
                use("attnT", mm_t[bi])
            x1s = T["x1t"][:, sl * D:(sl + 1) * D]
            t_x1 = None
            for i in range(4):
                t_x1 = prod(f"x1t{sl}", "v", lambda e, i=i, sl=sl, x1s=x1s:
                            e.tensor_tensor(
                                x1s[:, i * 512:(i + 1) * 512], pb[4 + i][:, 0:512],
                                T["xBs"][:, sl * D + i * 512: sl * D + (i + 1) * 512],
                                ALU.add),
                            deps=[mm_t[4 + i], t_xb])
                bk_use(4 + i, t_x1)
                use(f"xBs{sl}", t_x1)
            st = T["ln2st"][:, 0:24]
            stm = st.rearrange("p (c k) -> p c k", k=3)[:, :, 1:2].squeeze(2)
            stv = st.rearrange("p (c k) -> p c k", k=3)[:, :, 2:3].squeeze(2)
            msq = T["ln2st"][:, 24:40]
            col = T["ln2col"][:, 0:4]
            t_bn = None
            for bi_ in range(4):
                t_bn = S.op("v", lambda e, st=st, x1s=x1s, bi_=bi_: e.bn_stats(
                    st[:, bi_ * 6:(bi_ + 1) * 6],
                    x1s[:, bi_ * 512:(bi_ + 1) * 512]),
                    deps=[t_x1, dep("ln2war")])
            t1 = S.op("v", lambda e, stm=stm, col=col: e.tensor_reduce(
                col[:, 0:1], stm, AXX, ALU.add), deps=[t_bn])
            t2 = S.op("v", lambda e, stv=stv, col=col: e.tensor_reduce(
                col[:, 3:4], stv, AXX, ALU.add), deps=[t_bn])
            t3 = S.op("v", lambda e, stm=stm, msq=msq: e.tensor_tensor(
                msq[:, 0:8], stm, stm, ALU.mult), deps=[t_bn])
            t4 = S.op("v", lambda e, msq=msq: e.tensor_reduce(
                msq[:, 0:1], msq[:, 0:8], AXX, ALU.add), deps=[t3])
            t5 = S.op("v", lambda e, col=col: e.tensor_scalar(
                col[:, 0:1], col[:, 0:1], 1.0 / 8, None, ALU.mult), deps=[t1])
            t7 = S.op("v", lambda e, col=col: e.tensor_scalar(
                col[:, 3:4], col[:, 3:4], 1.0 / D, None, ALU.mult), deps=[t2])
            t8 = S.op("v", lambda e, msq=msq: e.tensor_scalar(
                msq[:, 0:1], msq[:, 0:1], 1.0 / 8, None, ALU.mult), deps=[t4])
            t9 = S.op("v", lambda e, col=col, msq=msq: e.tensor_tensor(
                col[:, 3:4], col[:, 3:4], msq[:, 0:1], ALU.add), deps=[t7, t8])
            t10 = S.op("v", lambda e, col=col: e.tensor_tensor(
                col[:, 1:2], col[:, 0:1], col[:, 0:1], ALU.mult), deps=[t5])
            t11 = S.op("v", lambda e, col=col: e.tensor_tensor(
                col[:, 3:4], col[:, 3:4], col[:, 1:2], ALU.subtract),
                deps=[t9, t10])
            t12 = S.op("s", lambda e, col=col: e.activation(
                col[:, 1:2], col[:, 3:4], AF.Sqrt, bias=T["epsln"][:, 0:1]),
                deps=[t11] + fb("s"))
            t13 = S.op("v", lambda e, col=col: e.reciprocal(
                col[:, 1:2], col[:, 1:2]), deps=[t12])
            t14 = S.op("v", lambda e, col=col: e.tensor_tensor(
                col[:, 2:3], col[:, 0:1], col[:, 1:2], ALU.mult), deps=[t13])
            t15 = S.op("v", lambda e, col=col: e.tensor_scalar(
                col[:, 2:3], col[:, 2:3], -1.0, None, ALU.mult), deps=[t14])
            t_l2 = S.op("s", lambda e, x1s=x1s, col=col: e.activation(
                T["ln2s"][:], x1s, AF.Identity,
                bias=col[:, 2:3], scale=col[:, 1:2]),
                deps=[t13, t15, dep("ln2sw")])
            last["ln2war"] = t_l2
            mm_t2 = {}
            for kt in range(KT):
                bi = 4 + kt // 4
                mm_t2[bi] = S.op("t", lambda e, kt=kt, bi=bi: e.matmul(
                    pb[bi][:, (kt % 4) * 128:(kt % 4 + 1) * 128],
                    T["ln2s"][:, kt * 128:(kt + 1) * 128],
                    T["ident"][:], start=True, stop=True),
                    deps=[t_l2] + (bk_war(bi) if kt % 4 == 0 else []))
            last["ln2sw"] = mm_t2[7]
            for i in range(4):
                ev = prod("ln2t", "v", lambda e, i=i, tg=tg: e.tensor_copy(
                    T["ln2t"][:].rearrange("p (k z) -> p k z", z=TB)[
                        :, 4 * i:4 * i + 4, tg * 128:(tg + 1) * 128],
                    pb[4 + i][:, 0:512]), deps=[mm_t2[4 + i]])
                bk_use(4 + i, ev)
            t_sp = S.op("q", lambda e, tb=tb, tg=tg, x1s=x1s: e.dma_start(
                x1sp[tb * TB + tg * 128: tb * TB + (tg + 1) * 128, :], x1s),
                deps=[t_x1])
            use(f"x1t{sl}", t_sp)
            x1sp_tok[(tb, tg)] = t_sp
        for hg in range(HG):
            sl = hg % 2
            t_wl = prod(f"wups{sl}", "q", lambda e, hg=hg, sl=sl: e.dma_start(
                T["wups"][:, sl * 2048:(sl + 1) * 2048].rearrange(
                    "p (r z f) -> p r z f", z=2, f=128),
                wup_ag[:].rearrange("(r z w) (p f) -> p r z w f", z=2, w=64, p=128)[
                    :, :, :, hg, :]),
                deps=up_toks)
            mm_l = [None, None]
            for kt in range(KT):
                for half in range(2):
                    bi = 2 * sl + half
                    mm_l[half] = S.op("t", lambda e, kt=kt, half=half, sl=sl, bi=bi:
                                      e.matmul(
                                          pb[bi][:, 0:512],
                                          T["wups"][:, sl * 2048 + kt * 128: sl * 2048 + (kt + 1) * 128],
                                          T["ln2t"][:, kt * TB + half * 512: kt * TB + (half + 1) * 512],
                                          start=(kt == 0), stop=(kt == KT - 1)),
                                      deps=([t_wl, dep("ln2t")]
                                            + (bk_war(bi) if kt == 0 else [])))
            use(f"wups{sl}", mm_l[1])
            use("ln2t", mm_l[1])
            x2sl = T["x2o"][:, sl * TB:(sl + 1) * TB]
            ev = [None, None]
            for half in range(2):
                bi = 2 * sl + half
                ev[half] = prod(f"x2o{sl}", "s", lambda e, hg=hg, half=half,
                                x2sl=x2sl, bi=bi: e.activation(
                                    x2sl[:, half * 512:(half + 1) * 512],
                                    pb[bi][:, 0:512], AF.Relu,
                                    bias=T["bupc"][:, hg:hg + 1]),
                                deps=[mm_l[half]])
                bk_use(bi, ev[half])
            t_sp = S.op("q", lambda e, hg=hg, x2sl=x2sl: e.dma_start(
                x2sp[hg * 128:(hg + 1) * 128, :], x2sl),
                deps=[ev[0], ev[1], x2sp_r_tok.get(hg)])
            use(f"x2o{sl}", t_sp)
            x2sp_w_tok[hg] = t_sp
        for sw in range(4):
            dn_mm = {}
            for hk in range(HG):
                sl = hk % 2
                t_x2 = prod(f"x2i{sl}", "q", lambda e, hk=hk, sl=sl: e.dma_start(
                    T["x2i"][:, sl * TB:(sl + 1) * TB],
                    x2sp[hk * 128:(hk + 1) * 128, :]), deps=[x2sp_w_tok[hk]])
                x2sp_r_tok[hk] = t_x2
                t_wd = prod(f"wdns{sl}", "q", lambda e, hk=hk, sw=sw, sl=sl:
                            e.dma_start(
                                T["wdns"][:, sl * 512:(sl + 1) * 512].rearrange(
                                    "p (g f) -> p g f", f=128),
                                wdn_ag[(hk // 8) * 128 + (hk % 8) * 16 + 4 * sw:
                                       (hk // 8) * 128 + (hk % 8) * 16 + 4 * sw + 4,
                                       :].rearrange("g (p f) -> p g f", p=128)),
                            deps=dn_toks)
                for j in range(4):
                    for half in range(2):
                        bi = 2 * j + half
                        dn_mm[bi] = S.op("t", lambda e, j=j, half=half, sl=sl, hk=hk, bi=bi:
                                         e.matmul(
                                             pb[bi][:, 0:512],
                                             T["wdns"][:, sl * 512 + j * 128: sl * 512 + (j + 1) * 128],
                                             T["x2i"][:, sl * TB + half * 512: sl * TB + (half + 1) * 512],
                                             start=(hk == 0), stop=(hk == HG - 1)),
                                         deps=([t_wd, t_x2]
                                               + (bk_war(bi) if hk == 0 else [])))
                use(f"x2i{sl}", dn_mm[7])
                use(f"wdns{sl}", dn_mm[7])
            for j in range(4):
                dg = 4 * sw + j
                for half in range(2):
                    bi = 2 * j + half
                    ev = prod("attnT", "s", lambda e, dg=dg, half=half, bi=bi:
                              e.activation(
                                  T["attnT"][:, dg * TB + half * 512: dg * TB + (half + 1) * 512],
                                  pb[bi][:, 0:512], AF.Identity,
                                  bias=T["bdnc"][:, dg:dg + 1]),
                              deps=[dn_mm[bi]])
                    bk_use(bi, ev)
        for tg in range(8):
            sl = tg % 2
            t_xr = prod(f"x1r{sl}", "q", lambda e, tb=tb, tg=tg, sl=sl: e.dma_start(
                T["x1r"][:, sl * D:(sl + 1) * D],
                x1sp[tb * TB + tg * 128: tb * TB + (tg + 1) * 128, :]),
                deps=[x1sp_tok[(tb, tg)]])
            mm_t = {}
            for dg in range(DG):
                bi = 4 + dg // 4
                mm_t[bi] = S.op("t", lambda e, dg=dg, tg=tg, bi=bi: e.matmul(
                    pb[bi][:, (dg % 4) * 128:(dg % 4 + 1) * 128],
                    T["attnT"][:, dg * TB + tg * 128: dg * TB + (tg + 1) * 128],
                    T["ident"][:], start=True, stop=True),
                    deps=[dep("attnT")] + (bk_war(bi) if dg % 4 == 0 else []))
                use("attnT", mm_t[bi])
            ot = T["outt"][:, 0:D]
            t_o = None
            for i in range(4):
                t_o = prod("outt", "v", lambda e, i=i, sl=sl, ot=ot:
                           e.tensor_tensor(
                               ot[:, i * 512:(i + 1) * 512], pb[4 + i][:, 0:512],
                               T["x1r"][:, sl * D + i * 512: sl * D + (i + 1) * 512],
                               ALU.add),
                           deps=[mm_t[4 + i], t_xr])
                bk_use(4 + i, t_o)
                use(f"x1r{sl}", t_o)
            last_out = S.op("q", lambda e, tb=tb, tg=tg, ot=ot: e.dma_start(
                out_d[tb * TB + tg * 128: tb * TB + (tg + 1) * 128, :], ot),
                deps=[t_o])
            use("outt", last_out)

    S.op("t", lambda e: e.nop(),
         deps=[last_out] + S.qbar("q") + S.qbar("g"), inc=0)

    import contextlib
    sems = {}
    stack = contextlib.ExitStack()
    sem_names = ["t", "v", "s"] + [f"{e}{j}" for e in ("q", "g")
                                   for j in range(Sched.NSEM)]
    for ename in sem_names:
        sems[ename] = stack.enter_context(nc.semaphore(f"sem_{ename}"))
    blk = stack.enter_context(nc.Block())
    S.emit(nc, blk, sems)
    stack.close()
    return nc


_NC_CACHE = {}


def kernel(**inputs):
    in_maps = host_prep(inputs)
    if "prog" not in _NC_CACHE:
        _NC_CACHE["prog"] = build_program()
    nc = _NC_CACHE["prog"]
    res = run_bass_kernel_spmd(nc, in_maps, core_ids=list(range(NC)))
    out = np.concatenate([np.asarray(res.results[c]["out"]) for c in range(NC)],
                         axis=0)
    return out.reshape(B, N, D).astype(np.float32)
